# version 06 (final)
# baseline (speedup 1.0000x reference)
import numpy as np
import concourse.bass as bass
import concourse.mybir as mybir
import concourse.library_config as libcfg

F32 = mybir.dt.float32
BF16 = mybir.dt.bfloat16
I16 = mybir.dt.int16
I32 = mybir.dt.int32
U32 = mybir.dt.uint32
from concourse.bass_core_rust import IndirectOffsetOnAxis as IOA
AF = mybir.ActivationFunctionType
ALU = mybir.AluOpType
AL = mybir.AxisListType
EPS = 1e-5




class VG:
    _OPS = {"tensor_copy", "tensor_tensor", "tensor_add", "tensor_sub",
            "tensor_mul", "tensor_max", "tensor_scalar", "tensor_single_scalar",
            "tensor_scalar_add", "tensor_scalar_sub", "tensor_scalar_mul",
            "tensor_scalar_max", "tensor_scalar_min", "scalar_tensor_tensor",
            "tensor_tensor_scan", "tensor_reduce", "reduce_sum", "reduce_max",
            "reciprocal", "memset", "memzero", "select", "transpose", "max",
            "max_index", "copy_predicated", "tensor_relu"}

    DRAIN = 8

    def __init__(self, eng):
        self._e = eng
        self._hist = []

    @staticmethod
    def _tname(x):
        t = getattr(x, "tensor", None)
        if t is not None:
            return getattr(t, "name", None) or str(id(t))
        return None

    def _names(self, args, kwargs):
        out = set()
        for x in list(args) + list(kwargs.values()):
            n = VG._tname(x)
            if n:
                out.add(n)
        return out

    def _pad(self, cur):
        need = 0
        h = self._hist
        for age in range(1, min(VG.DRAIN, len(h)) + 1):
            if cur & h[-age]:
                need = VG.DRAIN - age + 1
                break
        for _ in range(need):
            self._e.engine_nop()
            h.append(frozenset())
        if len(h) > 2 * VG.DRAIN:
            del h[:len(h) - VG.DRAIN]

    def __getattr__(self, name):
        f = getattr(self._e, name)
        if name not in VG._OPS or not callable(f):
            return f
        def wrapped(*a, **k):
            cur = self._names(a, k)
            self._pad(cur)
            self._hist.append(frozenset(cur))
            return f(*a, **k)
        return wrapped

    def wait_ge(self, *a, **k):
        return self._e.wait_ge(*a, **k)

    def engine_nop(self):
        self._hist.append(frozenset())
        return self._e.engine_nop()


def default_params():
    return dict(T=2048, C=32, D=1024, S=128, E=8, LCAP=640, I2=4096, I=2048,
                NCORES=8, GROUP=4, CAPG=5120, HB=512)


def swz16(ids, rows=128):
    n = len(ids)
    cols = (n + 15) // 16
    a = np.zeros((rows, cols), np.int16)
    for j, v in enumerate(ids):
        a[j % 16, j // 16] = v
    return a


class B:
    def __init__(self, nc):
        self.nc = nc
        self.sv = {}
        self.sh = {}

    def sem(self, name):
        if name not in self.sh:
            self.sh[name] = self.nc.alloc_semaphore(name)
            self.sv[name] = 0
        return self.sh[name]

    def inc(self, instr, name, n=1):
        h = self.sem(name)
        instr.then_inc(h, n)
        self.sv[name] += n
        return self.sv[name]

    def val(self, name):
        self.sem(name)
        return self.sv[name]

    def wait(self, eng, name, v=None):
        eng.wait_ge(self.sem(name), self.sv[name] if v is None else v)


def build_program(P):
    T, C, D, S, E = P["T"], P["C"], P["D"], P["S"], P["E"]
    LCAP, I2, II, HB = P["LCAP"], P["I2"], P["I"], P["HB"]
    NCORES, GROUP, CAPG = P["NCORES"], P["GROUP"], P["CAPG"]
    KD, NT, NCH = D // 128, T // 128, T // C
    CPT = 128 // C
    NSEL, NSLOT = 2 * T, E * LCAP
    NSC = LCAP // 128
    NHB = I2 // (2 * HB)
    NSB = HB // 128
    KI = II // 128
    NGR = T // 512
    NB = D // 512
    NROWS = NSLOT // 16
    NQ = (NROWS + 127) // 128
    NSP = (NSLOT + 32 + 1535) // 1536
    assert T % 512 == 0 and C == 32 and LCAP % 128 == 0 and T >= 2 * D
    assert NSLOT % 256 == 0 and I2 == 2 * II and HB % 128 == 0

    nc = bass.Bass("TRN2", target_bir_lowering=False,
                   detect_race_conditions=False)
    b = B(nc)

    IN = {}
    for nm, shp in (("x", [T, D]), ("Wq", [D, S]), ("bq", [S]), ("Wk", [D, S]),
                    ("bk", [S]), ("Wv", [D, S]), ("bv", [S]), ("Wg", [D, S]),
                    ("bg", [S]), ("Wo", [S, D]), ("bo", [D]), ("gamma", [D]),
                    ("beta", [D]), ("Wge", [E, D]), ("W1", [E, D, I2]),
                    ("W2", [E, II, D]), ("mask3", [128, max(GROUP - 1, 1)]),
                    ("rankm", [NCORES, 1])):
        IN[nm] = nc.dram_tensor(nm, shp, F32, kind="ExternalInput")
    out_ext = nc.dram_tensor("out", [T, D], F32, kind="ExternalOutput")

    DR = {}
    DR["xbf"] = nc.dram_tensor("xbf_d", [T, D], BF16)
    DR["x2bf"] = nc.dram_tensor("x2bf_d", [T + 1, D], BF16)
    DR["x2f"] = nc.dram_tensor("x2f_d", [T, D], F32)
    DR["y"] = nc.dram_tensor("y_d", [T + 1, D], F32)
    DR["tab"] = nc.dram_tensor("tab_d", [NSLOT + 64, 2], F32)
    DR["ag1i"] = nc.dram_tensor("ag1i_d", [S + 1, S], F32)
    DR["ag1o"] = nc.dram_tensor("ag1o_d", [GROUP * (S + 1), S], F32)
    DR["ag2i"] = nc.dram_tensor("ag2i_d", [1, E], F32)
    DR["ag2o"] = nc.dram_tensor("ag2o_d", [NCORES, E], F32)

    CN = {}
    CN["ident"] = nc.inline_tensor(np.eye(128, dtype=np.float32), "identc")
    CN["maskC"] = nc.inline_tensor(np.triu(np.ones((C, C), np.float32)), "maskCc")
    CN["triuT"] = nc.inline_tensor(
        np.triu(np.ones((128, 128), np.float32), k=1), "triuTc")
    CN["ones1"] = nc.inline_tensor(np.ones((1, 128), np.float32), "ones1c")
    CN["iota8"] = nc.inline_tensor(
        np.tile(np.arange(E, dtype=np.float32), (128, 1)), "iota8c")
    CN["tokid"] = nc.inline_tensor(
        (np.arange(128, dtype=np.float32)[:, None]
         + 128.0 * np.arange(NT, dtype=np.float32)[None, :]), "tokidc")

    A = nc.alloc_sbuf_tensor
    H = {}
    H["ident"] = A("s_ident", [128, 128], F32)
    H["maskC"] = A("s_maskC", [C, C], F32)
    H["triuT"] = A("s_triuT", [128, 128], F32)
    H["ones1"] = A("s_ones1", [1, 128], F32)
    H["iota8"] = A("s_iota8", [128, E], F32)
    H["mask3"] = A("s_mask3", [128, max(GROUP - 1, 1)], F32)
    H["rankm"] = A("s_rankm", [NCORES, 1], F32)
    H["tokid"] = A("s_tokid", [128, NT], F32)
    for nm in ("bq", "bk", "nbg"):
        H[nm] = A("s_" + nm, [128, 1], F32)
    H["r_bv"] = A("sr_bv", [1, S], F32)
    H["w_q"] = A("w_q", [128, KD, S], BF16)
    H["w_k"] = A("w_k", [128, KD, S], BF16)
    H["w_v"] = A("w_v", [128, KD, S], BF16)
    H["w_g"] = A("w_g", [128, KD, S], BF16)
    H["w_o"] = A("w_o", [S, D], BF16)
    H["w_ge"] = A("w_ge", [128, KD, E], BF16)
    H["logits"] = A("logits", [128, NT, E], F32)
    H["pos"] = A("pos", [128, NT, E], F32)
    H["i12"] = A("i12", [128, NT, 2], F32)
    H["g12"] = A("g12", [128, NT, 2], F32)
    H["slots"] = A("slots", [128, NT, 2], F32)
    H["oh"] = A("oh", [128, 2, E], F32)
    H["sml"] = A("sml", [128, 16], F32)
    H["tot"] = A("tot", [1, E], F32)
    H["gpB"] = A("gpB", [128, E], F32)
    H["rmT"] = A("rmT", [NCORES, 128], F32)
    H["agld"] = A("agld", [NCORES, E], F32)
    H["tok32"] = A("tok32", [128, 64], I32)
    H["gate_bs"] = A("gate_bs", [128, 64], F32)
    H["x2f_st"] = A("x2f_st", [128, 2, D], F32)
    H["x2b_st"] = A("x2b_st", [128, 2, D], BF16)
    H["mu"] = A("mu", [128, 16], F32)
    H["mu_s"] = A("mu_s", [128, 8], F32)
    H["u8"] = A("u8", [128, 8], U32)

    GA1, GA2, GA3 = [], [], []
    def _galloc(lst, name, shp, dt=F32):
        g = nc.sbuf_tensor(name, shp, dt)
        h = g.__enter__()
        lst.append(g)
        H[name] = h
        return h
    def galloc(name, shp, dt=F32):
        return _galloc(GA1, name, shp, dt)
    def galloc2(name, shp, dt=F32):
        return _galloc(GA2, name, shp, dt)
    galloc("scr1", [128, T])
    galloc("M_all", [128, NCH * S])
    H["St_all"] = H["M_all"]
    galloc("colbuf", [128, 8 * NCH])
    galloc("sl32", [128, NT, 2], I32)
    galloc("pairs", [128, NT, 2, 2])
    galloc("slf", [128, 96])
    galloc("wge_raw", [E, D])
    for nm in ("bo", "gam", "bet"):
        galloc("r_" + nm, [1, D])
    galloc2("gamB", [128, D])
    galloc2("betB", [128, D])
    galloc2("qs", [128, T])
    galloc2("ks2", [128, T])
    galloc2("Scum", [128, T])
    galloc2("v_tok", [128, NT, S])
    galloc2("attn", [128, T])
    galloc2("attnb", [128, T], BF16)
    galloc2("sc_sb", [128, 2, C])
    galloc2("kT_sb", [128, 2, 128])
    galloc2("S_in", [128, S])
    galloc2("Seff", [128, 2, S])
    galloc2("x_st", [128, 2, D])
    galloc2("y_st", [128, 2, D])

    H["x_fm"] = A("x_fm", [128, KD, T], BF16, side="right", is_transpose_dest=True)
    if NCH * S >= 8192:
        H["grep"] = H["M_all"]
    else:
        H["grep"] = A("grep", [32, 8192], F32, side="right")

    PS = {}
    for nm in ("psA", "psB", "psC", "psD"):
        PS[nm] = nc.alloc_psum_tensor(nm, [128, 2, 512], F32)

    COL = {nm: i for i, nm in enumerate(
        ["bias", "d1", "d2", "hd", "Dall", "Dd2", "tmp", "tmp2"])}

    def col(nm, c0=0, n=1):
        i = COL[nm]
        return H["colbuf"][:, NCH * i + c0: NCH * i + c0 + n]

    def sap(h, off, dims):
        return bass.AP(h, off, dims)

    with nc.Block("b_load") as blk:
        @blk.sync
        def _(sy):
            for src, dst in ((CN["ident"], H["ident"]), (CN["maskC"], H["maskC"]),
                             (CN["triuT"], H["triuT"]), (CN["ones1"], H["ones1"]),
                             (CN["iota8"], H["iota8"]), (CN["tokid"], H["tokid"])):
                b.inc(sy.dma_start(out=dst[...], in_=src.ap()), "ld", 16)
            b.wait(sy, "wat", 96)
            for kc in range(KD):
                b.inc(sy.dma_start_transpose(
                    H["x_fm"][:, kc, :], DR["xbf"][0:T, 128 * kc:128 * (kc + 1)]),
                    "xfm", 16)
            b.inc(sy.dma_start(out=H["mask3"][...], in_=IN["mask3"].ap()), "ld", 16)
            b.inc(sy.dma_start(out=H["rankm"][...], in_=IN["rankm"].ap()), "ld", 16)
            b.inc(sy.dma_start(out=H["wge_raw"][...], in_=IN["Wge"].ap()), "ld", 16)
            for nm, src in (("bq", "bq"), ("bk", "bk"), ("nbg", "bg")):
                b.inc(sy.dma_start(out=H[nm][:, :],
                                   in_=IN[src].ap().rearrange("(s one) -> s one", one=1)), "ld", 16)
            for nm, src in (("r_bv", "bv"), ("r_bo", "bo"),
                            ("r_gam", "gamma"), ("r_bet", "beta")):
                b.inc(sy.dma_start(out=H[nm][:, :],
                                   in_=IN[src].ap().rearrange("(one s) -> one s", one=1)), "ld", 16)

        @blk.gpsimd
        def _(g):
            for w_sb, w_dr in (("w_q", "Wq"), ("w_k", "Wk"), ("w_v", "Wv"),
                               ("w_g", "Wg")):
                b.inc(g.dma_start(
                    out=H[w_sb][:, :, :],
                    in_=IN[w_dr].ap().rearrange("(c p) s -> p c s", p=128)),
                    "wat", 16)
            b.inc(g.dma_start(out=H["w_o"][:, :], in_=IN["Wo"][:, :]), "wat", 16)
            b.inc(g.dma_start(out=DR["xbf"][:, :], in_=IN["x"][:, :]), "wat", 16)

        @blk.vector
        def _(v):
            v = VG(v)
            b.wait(v, "ld")
            b.inc(v.tensor_scalar_mul(H["nbg"][:, :], H["nbg"][:, :], -1.0),
                  "nbg", 1)

    with nc.Block("b_bcast") as blk:
        @blk.tensor
        def _(t):
            b.wait(t, "ld")
            for k, (nm, hh) in enumerate([(n, h) for n in ("r_gam", "r_bet")
                                          for h in range(NB)]):
                if k >= 2:
                    t.wait_ge(b.sem("bce"), k - 1)
                mm = t.matmul(PS["psA"][:, k % 2, :], H["ones1"][:, :],
                              H[nm][:, 512 * hh:512 * (hh + 1)],
                              start=True, stop=True)
                b.inc(mm, "bcm", 1)

        @blk.scalar
        def _(sc):
            for k, (nm, hh) in enumerate([(n, h) for n in ("gamB", "betB")
                                          for h in range(NB)]):
                sc.wait_ge(b.sem("bcm"), k + 1)
                b.inc(sc.copy(H[nm][:, 512 * hh:512 * (hh + 1)],
                              PS["psA"][:, k % 2, :]), "bce", 1)

    with nc.Block("b_proj") as blk:
        @blk.tensor
        def _(t):
            b.wait(t, "xfm")
            b.wait(t, "wat")
            k = 0
            for wname in ("w_q", "w_k", "w_g"):
                for gr in range(NGR):
                    if k >= 2:
                        t.wait_ge(b.sem("pev"), k - 1)
                    for kc in range(KD):
                        mm = t.matmul(PS["psA"][:, k % 2, :], H[wname][:, kc, :],
                                      H["x_fm"][:, kc, 512 * gr:512 * (gr + 1)],
                                      start=(kc == 0), stop=(kc == KD - 1))
                    b.inc(mm, "pmm", 1)
                    k += 1
            for tt in range(NT):
                if tt >= 2:
                    t.wait_ge(b.sem("vev"), tt - 1)
                for kc in range(KD):
                    t.matmul(PS["psB"][:, tt % 2, 0:S],
                             H["x_fm"][:, kc, 128 * tt:128 * (tt + 1)],
                             H["w_v"][:, kc, :], start=(kc == 0), stop=False)
                mm = t.matmul(PS["psB"][:, tt % 2, 0:S], H["ones1"][:, :],
                              H["r_bv"][:, :], start=False, stop=True)
                b.inc(mm, "vmm", 1)

        @blk.scalar
        def _(sc):
            b.wait(sc, "nbg")
            k = 0
            for outname, bias in (("qs", "bq"), ("ks2", "bk"), ("Scum", "nbg")):
                for gr in range(NGR):
                    sc.wait_ge(b.sem("pmm"), k + 1)
                    sl = slice(512 * gr, 512 * (gr + 1))
                    if outname == "Scum":
                        sc.activation(H["Scum"][:, sl], PS["psA"][:, k % 2, :],
                                      AF.Exp, bias=H["nbg"][:, 0:1], scale=-1.0)
                        b.inc(sc.activation(H["Scum"][:, sl], H["Scum"][:, sl],
                                            AF.Ln, bias=1.0), "pev", 1)
                    else:
                        b.inc(sc.activation(H[outname][:, sl],
                                            PS["psA"][:, k % 2, :], AF.Identity,
                                            bias=H[bias][:, 0:1]), "pev", 1)
                    k += 1

        @blk.vector
        def _(v):
            v = VG(v)
            for tt in range(NT):
                v.wait_ge(b.sem("vmm"), tt + 1)
                b.inc(v.tensor_copy(H["v_tok"][:, tt, :], PS["psB"][:, tt % 2, 0:S]),
                      "vev", 1)

    with nc.Block("b_gate") as blk:
        @blk.vector
        def _(v):
            v = VG(v)
            v.wait_ge(b.sem("pev"), 3 * NGR)
            b.inc(v.tensor_tensor_scan(H["Scum"][:, :], H["Scum"][:, :],
                                       H["Scum"][:, :], 0.0, ALU.add, ALU.bypass),
                  "scum", 1)
            v.memset(col("tmp", 0, NCH), 0.0)
            if NCH > 1:
                v.tensor_copy(col("tmp", 1, NCH - 1),
                              sap(H["Scum"], C - 1, [[T, 128], [C, NCH - 1]]))
            v.tensor_copy(col("tmp2", 0, NCH),
                          sap(H["Scum"], C - 1, [[T, 128], [C, NCH]]))
            v.tensor_add(col("bias", 0, NCH), col("tmp", 0, NCH),
                         col("tmp2", 0, NCH))
            v.tensor_scalar_mul(col("bias", 0, NCH), col("bias", 0, NCH), 0.5)
            v.tensor_sub(col("hd", 0, NCH), col("tmp", 0, NCH), col("tmp2", 0, NCH))
            b.inc(v.tensor_scalar_mul(col("hd", 0, NCH), col("hd", 0, NCH), 0.5),
                  "cols", 1)
            bias_b = sap(H["colbuf"], NCH * COL["bias"],
                         [[8 * NCH, 128], [1, NCH], [0, C]])
            b.inc(v.tensor_tensor(H["scr1"][:, :], H["Scum"][:, :], bias_b,
                                  ALU.subtract), "cols", 1)

        @blk.scalar
        def _(sc):
            sc.wait_ge(b.sem("cols"), 1)
            sc.activation(col("d2", 0, NCH), col("hd", 0, NCH), AF.Exp)
            b.inc(sc.activation(col("Dall", 0, NCH), col("tmp2", 0, NCH),
                                AF.Exp, scale=-1.0), "dcol", 1)
            sc.wait_ge(b.sem("cols"), 2)
            b.inc(sc.activation(H["scr1"][:, :], H["scr1"][:, :], AF.Exp),
                  "dcol", 1)

    with nc.Block("b_gate2") as blk:
        @blk.vector
        def _(v):
            v = VG(v)
            v.wait_ge(b.sem("dcol"), 1)
            v.tensor_mul(col("d1", 0, NCH), col("d2", 0, NCH), col("d2", 0, NCH))
            if NCH > 1:
                v.tensor_mul(col("Dd2", 1, NCH - 1), col("Dall", 0, NCH - 1),
                             col("d2", 1, NCH - 1))
            v.tensor_copy(col("Dd2", 0, 1), col("d2", 0, 1))
            v.wait_ge(b.sem("dcol"), 2)
            b.inc(v.tensor_mul(H["ks2"][:, :], H["ks2"][:, :], H["scr1"][:, :]),
                  "ksq", 1)
            v.reciprocal(H["scr1"][:, :], H["scr1"][:, :])
            b.inc(v.tensor_mul(H["qs"][:, :], H["qs"][:, :], H["scr1"][:, :]),
                  "ksq", 1)

    with nc.Block("b_chunks") as blk:
        @blk.tensor
        def _(t):
            b.wait(t, "ksq")
            t.wait_ge(b.sem("vev"), NT)
            for c in range(NCH):
                pb = 32 * (c % CPT)
                r = c % 2
                ch = slice(C * c, C * (c + 1))
                if c >= 2:
                    t.wait_ge(b.sem("ev1"), c - 1)
                b.inc(t.matmul(PS["psA"][0:C, r, 0:C], H["ks2"][:, ch],
                               H["qs"][:, ch], start=True, stop=True), "t1", 1)
                if c >= 2:
                    t.wait_ge(b.sem("ev2"), c - 1)
                b.inc(t.matmul(PS["psB"][0:C, r, 0:128], H["ks2"][:, ch],
                               H["ident"][:, :], start=True, stop=True),
                      "t2", 1)
                t.wait_ge(b.sem("ev2"), c + 1)
                if c >= 2:
                    t.wait_ge(b.sem("ev3"), c - 1)
                b.inc(t.matmul(PS["psC"][:, r, 0:128],
                               H["kT_sb"][pb:pb + C, r, :],
                               H["v_tok"][pb:pb + C, c // CPT, :],
                               start=True, stop=True, tile_position=(pb, 0)),
                      "t3", 1)
                t.wait_ge(b.sem("ev1"), c + 1)
                if c >= 2:
                    t.wait_ge(b.sem("ev4"), c - 1)
                b.inc(t.matmul(PS["psD"][:, r, 0:C],
                               H["v_tok"][pb:pb + C, c // CPT, :],
                               H["sc_sb"][pb:pb + C, r, :],
                               start=True, stop=True, tile_position=(pb, 0)),
                      "t4", 1)

        @blk.vector
        def _(v):
            v = VG(v)
            for c in range(NCH):
                pb = 32 * (c % CPT)
                r = c % 2
                v.wait_ge(b.sem("t1"), c + 1)
                b.inc(v.tensor_tensor(H["sc_sb"][pb:pb + C, r, :],
                                      PS["psA"][0:C, r, 0:C], H["maskC"][:, :],
                                      ALU.mult), "ev1", 1)
                v.wait_ge(b.sem("t3"), c + 1)
                b.inc(v.tensor_scalar_mul(H["M_all"][:, S * c:S * (c + 1)],
                                          PS["psC"][:, r, 0:128],
                                          col("d2", c, 1)), "ev3", 1)

        @blk.scalar
        def _(sc):
            for c in range(NCH):
                pb = 32 * (c % CPT)
                r = c % 2
                sc.wait_ge(b.sem("t2"), c + 1)
                b.inc(sc.copy(H["kT_sb"][pb:pb + C, r, :],
                              PS["psB"][0:C, r, 0:128]), "ev2", 1)
                sc.wait_ge(b.sem("t4"), c + 1)
                b.inc(sc.copy(H["attn"][:, C * c:C * (c + 1)],
                              PS["psD"][:, r, 0:C]), "ev4", 1)

    with nc.Block("b_scan") as blk:
        @blk.vector
        def _(v):
            v = VG(v)
            v.wait_ge(b.sem("ev3"), NCH)
            for j in range(S):
                i = v.tensor_tensor_scan(
                    sap(H["St_all"], j, [[NCH * S, 128], [S, NCH]]),
                    col("d1", 0, NCH),
                    sap(H["M_all"], j, [[NCH * S, 128], [S, NCH]]),
                    0.0, ALU.mult, ALU.add)
            b.inc(i, "scan", 1)

        @blk.scalar
        def _(sc):
            b.wait(sc, "scum")
            b.inc(sc.activation(col("tmp", 0, 1), H["Scum"][:, T - 1:T],
                                AF.Exp, scale=-1.0), "aseg", 1)

        @blk.sync
        def _(sy):
            sy.wait_ge(b.sem("scan"), 1)
            b.inc(sy.dma_start(out=DR["ag1i"][0:S, :],
                               in_=H["St_all"][:, S * (NCH - 1):S * NCH]),
                  "ag1w", 16)
            sy.wait_ge(b.sem("aseg"), 1)
            b.inc(sy.dma_start(out=bass.AP(DR["ag1i"], S * S, [[1, 128], [1, 1]]),
                               in_=col("tmp", 0, 1)), "ag1w", 16)

        @blk.gpsimd
        def _(g):
            b.wait(g, "ag1w")
            i = g.collective_compute(
                "AllGather", ALU.bypass,
                replica_groups=[list(range(gg * GROUP, (gg + 1) * GROUP))
                                for gg in range(NCORES // GROUP)],
                ins=[DR["ag1i"].ap()], outs=[DR["ag1o"].ap()])
            b.inc(i, "ag1", 1)

    with nc.Block("b_sin") as blk:
        @blk.sync
        def _(sy):
            b.wait(sy, "ag1")
            for j in range(GROUP - 1):
                base = j * (S + 1)
                if j >= 2:
                    sy.wait_ge(b.sem("sinj"), j - 1)
                b.inc(sy.dma_start(out=H["kT_sb"][:, j % 2, :],
                                   in_=DR["ag1o"][base:base + S, :]), "mld", 16)
                b.inc(sy.dma_start(
                    out=sap(H["colbuf"], NCH * COL["tmp2"] + j,
                            [[8 * NCH, 128], [1, 1]]),
                    in_=bass.AP(DR["ag1o"], (base + S) * S, [[1, 128], [1, 1]])),
                    "mld", 16)

        @blk.vector
        def _(v):
            v = VG(v)
            v.memset(H["S_in"][:, :], 0.0)
            for j in range(GROUP - 1):
                v.wait_ge(b.sem("mld"), 32 * (j + 1))
                mj = H["mask3"][:, j:j + 1]
                aj = col("tmp2", j, 1)
                v.tensor_scalar_add(col("tmp", j, 1), aj, -1.0)
                v.tensor_mul(col("tmp", j, 1), col("tmp", j, 1), mj)
                v.tensor_scalar_add(col("tmp", j, 1), col("tmp", j, 1), 1.0)
                v.tensor_scalar_mul(H["S_in"][:, :], H["S_in"][:, :],
                                    col("tmp", j, 1))
                v.tensor_scalar_mul(H["Seff"][:, 0, :], H["kT_sb"][:, j % 2, :], mj)
                b.inc(v.tensor_add(H["S_in"][:, :], H["S_in"][:, :],
                                   H["Seff"][:, 0, :]), "sinj", 1)
            b.inc(v.engine_nop(), "sin", 1)

    with nc.Block("b_pass2") as blk:
        @blk.scalar
        def _(sc):
            b.wait(sc, "sin")
            for c in range(NCH):
                r = c % 2
                if c == 0:
                    i = sc.mul(H["Seff"][:, r, :], H["S_in"][:, :], col("d2", 0, 1))
                else:
                    if c >= 2:
                        sc.wait_ge(b.sem("p2m"), c - 1)
                    i = sc.mul(H["Seff"][:, r, :],
                               H["St_all"][:, S * (c - 1):S * c], col("d2", c, 1))
                b.inc(i, "sef1", 1)

        @blk.vector
        def _(v):
            v = VG(v)
            for c in range(NCH):
                r = c % 2
                v.wait_ge(b.sem("sef1"), c + 1)
                if c > 0:
                    i = v.scalar_tensor_tensor(H["Seff"][:, r, :], H["S_in"][:, :],
                                               col("Dd2", c, 1), H["Seff"][:, r, :],
                                               ALU.mult, ALU.add)
                    b.inc(i, "sef2", 1)
                else:
                    b.inc(v.engine_nop(), "sef2", 1)
                v.wait_ge(b.sem("p2t"), c + 1)
                b.inc(v.tensor_add(H["attn"][:, C * c:C * (c + 1)],
                                   H["attn"][:, C * c:C * (c + 1)],
                                   PS["psA"][:, c % 2, 0:C]), "p2m", 1)
            v.wait_ge(b.sem("p2m"), NCH)
            b.inc(v.tensor_copy(H["attnb"][:, :], H["attn"][:, :]), "atb", 1)

        @blk.tensor
        def _(t):
            for c in range(NCH):
                t.wait_ge(b.sem("sef2"), c + 1)
                if c >= 2:
                    t.wait_ge(b.sem("p2m"), c - 1)
                b.inc(t.matmul(PS["psA"][:, c % 2, 0:C], H["Seff"][:, c % 2, :],
                               H["qs"][:, C * c:C * (c + 1)],
                               start=True, stop=True), "p2t", 1)

    with nc.Block("b_y") as blk:
        @blk.sync
        def _(sy):
            def x2writes(tw):
                sy.wait_ge(b.sem("x2fs"), tw + 1)
                b.inc(sy.dma_start(out=DR["x2f"][128 * tw:128 * (tw + 1), :],
                                   in_=H["x2f_st"][:, tw % 2, :]), "x2w", 16)
                sy.wait_ge(b.sem("x2b"), tw + 1)
                b.inc(sy.dma_start(out=DR["x2bf"][128 * tw:128 * (tw + 1), :],
                                   in_=H["x2b_st"][:, tw % 2, :]), "x2w", 16)
            for tt in range(NT):
                r = tt % 2
                if tt >= 2:
                    sy.wait_ge(b.sem("wdone"), tt - 1)
                b.inc(sy.dma_start(out=H["x_st"][:, tt % 2, :],
                                   in_=IN["x"][128 * tt:128 * (tt + 1), :]),
                      "xld", 16)
                if tt >= 2:
                    x2writes(tt - 2)
                sy.wait_ge(b.sem("vs1"), tt + 1)
                b.inc(sy.dma_start(out=H["mu_s"][:, 0 + r:1 + r],
                                   in_=H["mu"][:, 6 + r:7 + r]), "vb1", 16)
                sy.wait_ge(b.sem("vs2"), tt + 1)
                b.inc(sy.dma_start(out=H["mu_s"][:, 4 + r:5 + r],
                                   in_=H["mu"][:, 14 + r:15 + r]), "vb2", 16)
            for tw in (NT - 2, NT - 1):
                x2writes(tw)

        @blk.tensor
        def _(t):
            b.wait(t, "atb")
            for tt in range(NT):
                t.wait_ge(b.sem("xld"), 16 * (tt + 1))
                for nb_ in range(NB):
                    q = NB * tt + nb_
                    if q >= 2:
                        t.wait_ge(b.sem("yev"), q - 1)
                    t.matmul(PS["psB"][:, q % 2, :],
                             H["attnb"][:, 128 * tt:128 * (tt + 1)],
                             H["w_o"][:, 512 * nb_:512 * (nb_ + 1)],
                             start=True, stop=False)
                    mm = t.matmul(PS["psB"][:, q % 2, :], H["ones1"][:, :],
                                  H["r_bo"][:, 512 * nb_:512 * (nb_ + 1)],
                                  start=False, stop=True)
                    b.inc(mm, "ymm", 1)

        @blk.vector
        def _(v):
            v = VG(v)
            for tt in range(NT):
                r = tt % 2
                for nb_ in range(NB):
                    q = NB * tt + nb_
                    v.wait_ge(b.sem("ymm"), q + 1)
                    sl = slice(512 * nb_, 512 * (nb_ + 1))
                    i = v.scalar_tensor_tensor(
                        H["y_st"][:, r, sl], PS["psB"][:, q % 2, :], 1.0,
                        H["x_st"][:, r, sl], ALU.mult, ALU.add)
                    b.inc(i, "yev", 1)
                v.reduce_sum(H["mu"][:, 4 + r:5 + r], H["y_st"][:, r, :],
                             axis=AL.X)
                v.tensor_scalar_mul(H["mu"][:, 4 + r:5 + r],
                                    H["mu"][:, 4 + r:5 + r], -1.0 / D)
                if tt >= 2:
                    v.wait_ge(b.sem("x2w"), 32 * (tt - 1))
                v.tensor_tensor(H["x2f_st"][:, r, :], H["y_st"][:, r, :],
                                H["y_st"][:, r, :], ALU.mult)
                v.reduce_sum(H["mu"][:, 8 + r:9 + r], H["x2f_st"][:, r, :],
                             axis=AL.X)
                v.tensor_scalar(H["mu"][:, 6 + r:7 + r],
                                H["mu"][:, 8 + r:9 + r], 1.0 / D, EPS,
                                ALU.mult, ALU.add)
                v.tensor_tensor(H["mu"][:, 0 + r:1 + r], H["mu"][:, 4 + r:5 + r],
                                H["mu"][:, 4 + r:5 + r], ALU.mult)
                i = v.tensor_tensor(H["mu"][:, 6 + r:7 + r],
                                    H["mu"][:, 6 + r:7 + r],
                                    H["mu"][:, 0 + r:1 + r], ALU.subtract)
                b.inc(i, "vs1", 1)
                c1 = H["scr1"][:, D * r:D * (r + 1)]
                v.tensor_scalar(c1, H["y_st"][:, r, :],
                                H["mu"][:, 4 + r:5 + r], None, ALU.add)
                v.wait_ge(b.sem("sqt1"), tt + 1)
                v.reciprocal(H["mu"][:, 10 + r:11 + r], H["mu_s"][:, 2 + r:3 + r])
                v.scalar_tensor_tensor(c1, c1, H["mu"][:, 10 + r:11 + r],
                                       H["gamB"][:, :], ALU.mult, ALU.mult)
                v.tensor_add(H["y_st"][:, r, :], H["x_st"][:, r, :],
                             H["betB"][:, :])
                i = v.scalar_tensor_tensor(
                    H["y_st"][:, r, :], c1, 1.0, H["y_st"][:, r, :],
                    ALU.mult, ALU.add)
                b.inc(i, "wdone", 1)
                v.reduce_sum(H["mu"][:, 12 + r:13 + r], H["y_st"][:, r, :],
                             axis=AL.X)
                v.tensor_scalar_mul(H["mu"][:, 12 + r:13 + r],
                                    H["mu"][:, 12 + r:13 + r], -1.0 / D)
                v.tensor_tensor(H["x2f_st"][:, r, :], H["y_st"][:, r, :],
                                H["y_st"][:, r, :], ALU.mult)
                v.reduce_sum(H["mu"][:, 8 + r:9 + r], H["x2f_st"][:, r, :],
                             axis=AL.X)
                v.tensor_scalar(H["mu"][:, 14 + r:15 + r],
                                H["mu"][:, 8 + r:9 + r], 1.0 / D, EPS,
                                ALU.mult, ALU.add)
                v.tensor_tensor(H["mu"][:, 0 + r:1 + r],
                                H["mu"][:, 12 + r:13 + r],
                                H["mu"][:, 12 + r:13 + r], ALU.mult)
                i = v.tensor_tensor(H["mu"][:, 14 + r:15 + r],
                                    H["mu"][:, 14 + r:15 + r],
                                    H["mu"][:, 0 + r:1 + r], ALU.subtract)
                b.inc(i, "vs2", 1)
                v.tensor_scalar(H["x2f_st"][:, r, :], H["y_st"][:, r, :],
                                H["mu"][:, 12 + r:13 + r], None, ALU.add)
                v.wait_ge(b.sem("sqt2"), tt + 1)
                v.reciprocal(H["mu"][:, 14 + r:15 + r], H["mu_s"][:, 6 + r:7 + r])
                i = v.tensor_scalar(H["x2f_st"][:, r, :], H["x2f_st"][:, r, :],
                                    H["mu"][:, 14 + r:15 + r], None, ALU.mult)
                b.inc(i, "x2fs", 1)
                b.inc(v.tensor_copy(H["x2b_st"][:, r, :], H["x2f_st"][:, r, :]),
                      "x2b", 1)

        @blk.scalar
        def _(sc):
            for tt in range(NT):
                r = tt % 2
                sc.wait_ge(b.sem("vb1"), 16 * (tt + 1))
                b.inc(sc.activation(H["mu_s"][:, 2 + r:3 + r],
                                    H["mu_s"][:, 0 + r:1 + r], AF.Sqrt),
                      "sqt1", 1)
                sc.wait_ge(b.sem("vb2"), 16 * (tt + 1))
                b.inc(sc.activation(H["mu_s"][:, 6 + r:7 + r],
                                    H["mu_s"][:, 4 + r:5 + r], AF.Sqrt),
                      "sqt2", 1)

    with nc.Block("b_r0") as blk:
        @blk.sync
        def _(sy):
            b.wait(sy, "x2w")
            for kc in range(KD):
                b.inc(sy.dma_start_transpose(
                    H["x_fm"][:, kc, :],
                    DR["x2bf"][0:T, 128 * kc:128 * (kc + 1)]), "x2fm", 16)

        @blk.tensor
        def _(t):
            b.wait(t, "ld")
            for kc in range(KD):
                if kc >= 2:
                    t.wait_ge(b.sem("wgev"), kc - 1)
                b.inc(t.matmul(PS["psA"][0:128, kc % 2, 0:E],
                               H["wge_raw"][:, 128 * kc:128 * (kc + 1)],
                               H["ident"][0:E, 0:E], start=True, stop=True),
                      "wgt", 1)

        @blk.vector
        def _(v):
            v = VG(v)
            for kc in range(KD):
                v.wait_ge(b.sem("wgt"), kc + 1)
                b.inc(v.tensor_copy(H["w_ge"][:, kc, :], PS["psA"][0:128, kc % 2, 0:E]),
                      "wgev", 1)

    with nc.Block("b_r1") as blk:
        @blk.tensor
        def _(t):
            b.wait(t, "x2fm")
            t.wait_ge(b.sem("wgev"), KD)
            for tt in range(NT):
                if tt >= 2:
                    t.wait_ge(b.sem("lgev2"), tt - 1)
                for kc in range(KD):
                    mm = t.matmul(PS["psC"][:, tt % 2, 0:E],
                                  H["x_fm"][:, kc, 128 * tt:128 * (tt + 1)],
                                  H["w_ge"][:, kc, :],
                                  start=(kc == 0), stop=(kc == KD - 1))
                b.inc(mm, "lgT", 1)

        @blk.vector
        def _(v):
            v = VG(v)
            for tt in range(NT):
                v.wait_ge(b.sem("lgT"), tt + 1)
                b.inc(v.tensor_copy(H["logits"][:, tt, :],
                                    PS["psC"][:, tt % 2, 0:E]), "lgev2", 1)

    with nc.Block("b_r2") as blk:
        @blk.vector
        def _(v):
            v = VG(v)
            v.wait_ge(b.sem("lgev2"), NT)
            v.memset(H["sml"][:, 7:8], 1.0)
            b.inc(v.memset(H["tot"][:, :], 0.0), "totev", 1)
            for tt in range(NT):
                lg = H["logits"][:, tt, :]
                v.max(H["oh"][:, 0, :], lg)
                v.max_index(H["u8"][:, :], H["oh"][:, 0, :], lg)
                v.tensor_copy(H["i12"][:, tt, :], H["u8"][:, 0:2])
                v.tensor_sub(H["sml"][:, 0:1], H["oh"][:, 0, 0:1],
                             H["oh"][:, 0, 1:2])
                b.inc(v.engine_nop(), "tk", 1)
                v.wait_ge(b.sem("sgm"), 2 * tt + 2)
                v.tensor_single_scalar(H["oh"][:, 0, :], H["iota8"][:, :],
                                       H["i12"][:, tt, 0:1], ALU.is_equal)
                v.tensor_single_scalar(H["oh"][:, 1, :], H["iota8"][:, :],
                                       H["i12"][:, tt, 1:2], ALU.is_equal)
                b.inc(v.tensor_add(H["pos"][:, tt, :], H["oh"][:, 0, :],
                                   H["oh"][:, 1, :]), "cnt", 1)

        @blk.scalar
        def _(sc):
            for tt in range(NT):
                sc.wait_ge(b.sem("tk"), tt + 1)
                b.inc(sc.activation(H["g12"][:, tt, 0:1], H["sml"][:, 0:1],
                                    AF.Sigmoid), "sgm", 1)
                b.inc(sc.activation(H["g12"][:, tt, 1:2], H["sml"][:, 0:1],
                                    AF.Sigmoid, scale=-1.0), "sgm", 1)

    with nc.Block("b_r3") as blk:
        @blk.tensor
        def _(t):
            b.wait(t, "cnt", b.val("cnt"))
            for tt in range(NT):
                t.wait_ge(b.sem("cnt"), tt + 1)
                t.wait_ge(b.sem("totev"), tt + 1)
                if tt >= 2:
                    t.wait_ge(b.sem("posev"), tt - 1)
                t.matmul(PS["psD"][0:128, tt % 2, 0:E], H["triuT"][:, :],
                         H["pos"][:, tt, :], start=True, stop=False)
                b.inc(t.matmul(PS["psD"][0:128, tt % 2, 0:E], H["ones1"][:, :],
                               H["tot"][:, :], start=False, stop=True), "posm", 1)
                b.inc(t.matmul(PS["psA"][0:1, 0, 0:E], H["sml"][:, 7:8],
                               H["pos"][:, tt, :],
                               start=True, stop=True), "summ", 1)

        @blk.vector
        def _(v):
            v = VG(v)
            for tt in range(NT):
                v.wait_ge(b.sem("summ"), tt + 1)
                b.inc(v.tensor_add(H["tot"][:, :], H["tot"][:, :],
                                   PS["psA"][0:1, 0, 0:E]), "totev", 1)
                v.wait_ge(b.sem("summ"), tt + 1)
                b.inc(v.tensor_copy(H["pos"][:, tt, :],
                                    PS["psD"][0:128, tt % 2, 0:E]), "posev", 1)

    with nc.Block("b_r4") as blk:
        @blk.sync
        def _(sy):
            sy.wait_ge(b.sem("totev"), NT)
            b.inc(sy.dma_start(out=DR["ag2i"][:, :], in_=H["tot"][:, :]),
                  "ag2w", 16)

        @blk.gpsimd
        def _(g):
            b.wait(g, "ag2w")
            b.inc(g.collective_compute(
                "AllGather", ALU.bypass,
                replica_groups=[list(range(NCORES))],
                ins=[DR["ag2i"].ap()], outs=[DR["ag2o"].ap()]), "ag2", 1)

    with nc.Block("b_r4b") as blk:
        @blk.sync
        def _(sy):
            b.wait(sy, "ag2")
            b.inc(sy.dma_start(out=H["agld"][:, :], in_=DR["ag2o"][:, :]),
                  "agld", 16)

        @blk.vector
        def _(v):
            v = VG(v)
            b.wait(v, "agld")
            v.memset(H["rmT"][:, :], 1.0)
            b.inc(v.tensor_scalar_mul(H["rmT"][:, :], H["rmT"][:, :],
                                      H["rankm"][:, 0:1]), "rmt", 1)

        @blk.tensor
        def _(t):
            b.wait(t, "rmt")
            b.inc(t.matmul(PS["psA"][0:128, 1, 0:E], H["rmT"][:, :],
                           H["agld"][:, :], start=True, stop=True), "gpm", 1)

        @blk.scalar
        def _(sc):
            b.wait(sc, "gpm")
            b.inc(sc.copy(H["gpB"][:, :], PS["psA"][0:128, 1, 0:E]), "gpe", 1)

    with nc.Block("b_r5") as blk:
        @blk.vector
        def _(v):
            v = VG(v)
            b.wait(v, "gpe")
            for tt in range(NT):
                for k in range(2):
                    ii = H["i12"][:, tt, k:k + 1]
                    v.tensor_single_scalar(H["oh"][:, 0, :], H["iota8"][:, :],
                                           ii, ALU.is_equal)
                    v.tensor_mul(H["oh"][:, 1, :], H["pos"][:, tt, :],
                                 H["oh"][:, 0, :])
                    v.reduce_sum(H["sml"][:, 2:3], H["oh"][:, 1, :],
                                 mybir.AxisListType.X)
                    v.tensor_mul(H["oh"][:, 1, :], H["gpB"][:, :], H["oh"][:, 0, :])
                    v.reduce_sum(H["sml"][:, 3:4], H["oh"][:, 1, :],
                                 mybir.AxisListType.X)
                    v.tensor_add(H["sml"][:, 4:5], H["sml"][:, 2:3],
                                 H["sml"][:, 3:4])
                    v.tensor_single_scalar(H["sml"][:, 4:5], H["sml"][:, 4:5],
                                           float(CAPG), ALU.is_lt)
                    v.tensor_mul(H["g12"][:, tt, k:k + 1], H["g12"][:, tt, k:k + 1],
                                 H["sml"][:, 4:5])
                    v.tensor_single_scalar(H["sml"][:, 5:6], ii, float(LCAP),
                                           ALU.mult)
                    v.tensor_add(H["sml"][:, 5:6], H["sml"][:, 5:6],
                                 H["sml"][:, 2:3])
                    v.tensor_single_scalar(H["sml"][:, 6:7], H["sml"][:, 2:3],
                                           float(LCAP), ALU.is_ge)
                    v.tensor_scalar(H["sml"][:, 1:2], H["sml"][:, 5:6], -1.0,
                                    float(NSLOT + 2 * tt + k), ALU.mult, ALU.add)
                    v.tensor_mul(H["sml"][:, 1:2], H["sml"][:, 1:2],
                                 H["sml"][:, 6:7])
                    v.tensor_add(H["slots"][:, tt, k:k + 1], H["sml"][:, 5:6],
                                 H["sml"][:, 1:2])
                b.inc(v.engine_nop(), "slot", 1)

    with nc.Block("b_r10") as blk:
        @blk.vector
        def _(v):
            v = VG(v)
            v.wait_ge(b.sem("wdone"), NT)
            b.inc(v.memset(H["x_st"][:, :, :], 0.0), "yz0", 1)

        @blk.sync
        def _(sy):
            b.wait(sy, "yz0")
            for tt in range(NT):
                b.inc(sy.dma_start(out=DR["y"][128 * tt:128 * (tt + 1), :],
                                   in_=H["x_st"][:, tt % 2, :]), "yzw", 16)
            b.inc(sy.dma_start(out=DR["y"][T:T + 1, :],
                               in_=H["x_st"][0:1, 0, :]), "yzw", 16)

        @blk.gpsimd
        def _(g):
            b.wait(g, "yz0")
            b.inc(g.dma_start(out=DR["x2bf"][T:T + 1, :],
                              in_=H["x_st"][0:1, 0, :]), "wat", 16)

    for g_ in reversed(GA2):
        g_.__exit__(None, None, None)
    _galloc(GA3, "L1", [64, 256])
    _galloc(GA3, "tokL", [64, 128])
    _galloc(GA3, "gateL", [64, 128])
    _galloc(GA3, "fillt", [81, 128])
    NSLOT64 = NSLOT + 64
    with nc.Block("b_rt1") as blk:
        @blk.vector
        def _(v):
            v = VG(v)
            b.wait(v, "yzw")
            v.wait_ge(b.sem("slot"), NT)
            v.tensor_copy(
                sap(H["pairs"], 0, [[4 * NT, 128], [4, NT], [2, 2]]),
                sap(H["tokid"], 0, [[NT, 128], [1, NT], [0, 2]]))
            v.tensor_copy(
                sap(H["pairs"], 1, [[4 * NT, 128], [4, NT], [2, 2]]),
                sap(H["g12"], 0, [[2 * NT, 128], [2, NT], [1, 2]]))
            b.inc(v.tensor_copy(H["sl32"][:, :, :], H["slots"][:, :, :]),
                  "prq", 1)
            v.memset(sap(H["fillt"], 0, [[128, 81], [2, 64]]), float(T))
            b.inc(v.memset(sap(H["fillt"], 1, [[128, 81], [2, 64]]), 0.0),
                  "prq", 1)

        @blk.sync
        def _(sy):
            sy.wait_ge(b.sem("prq"), 2)
            b.inc(sy.dma_start(
                out=bass.AP(DR["tab"], 0, [[128, (NSLOT64 * 2) // 128], [1, 128]]),
                in_=H["fillt"][0:(NSLOT64 * 2) // 128, :]), "tabp", 16)

        @blk.gpsimd
        def _(g):
            b.wait(g, "tabp")
            g.wait_ge(b.sem("prq"), 2)
            for tt in range(NT):
                for k in range(2):
                    b.inc(g.indirect_dma_start(
                        DR["tab"].ap(), IOA(H["sl32"][:, tt, k:k + 1], 0),
                        H["pairs"][:, tt, k, :], None), "tsc", 16)

    with nc.Block("b_rt2") as blk:
        @blk.sync
        def _(sy):
            b.wait(sy, "tsc")
            b.inc(sy.dma_start(
                out=H["L1"][0:(NSLOT * 2) // 256, :],
                in_=bass.AP(DR["tab"], 0, [[256, (NSLOT * 2) // 256], [1, 256]])),
                "tabr", 16)

        @blk.vector
        def _(v):
            v = VG(v)
            v.memset(H["tokL"][:, :], 0.0)
            v.memset(H["gateL"][:, :], 0.0)
            b.wait(v, "tabr")
            nr = (NSLOT * 2) // 256
            v.tensor_copy(H["tokL"][0:nr, :],
                          sap(H["L1"], 0, [[256, nr], [2, 128]]))
            b.inc(v.tensor_copy(H["gateL"][0:nr, :],
                                sap(H["L1"], 1, [[256, nr], [2, 128]])), "tspl", 1)

        @blk.tensor
        def _(t):
            b.wait(t, "tspl")
            b.inc(t.matmul(PS["psA"][0:128, 0, 0:64], H["tokL"][:, :],
                           H["ident"][0:64, 0:64], start=True, stop=True),
                  "tokT", 1)
            b.inc(t.matmul(PS["psB"][0:128, 0, 0:64], H["gateL"][:, :],
                           H["ident"][0:64, 0:64], start=True, stop=True),
                  "tokT", 1)

        @blk.scalar
        def _(sc):
            sc.wait_ge(b.sem("tokT"), 2)
            sc.copy(H["tok32"][:, :], PS["psA"][0:128, 0, 0:64])
            b.inc(sc.copy(H["gate_bs"][:, :], PS["psB"][0:128, 0, 0:64]),
                  "lists", 1)

    for g_ in reversed(GA3):
        g_.__exit__(None, None, None)
    for g_ in reversed(GA1):
        g_.__exit__(None, None, None)
    GB = []
    def balloc(name, shp, dt=F32, tdest=False):
        g_ = nc.sbuf_tensor(name, shp, dt, is_transpose_dest=tdest)
        h_ = g_.__enter__()
        GB.append(g_)
        H[name] = h_
        return h_
    balloc("batch", [128, 2, KD, LCAP], BF16, tdest=True)
    balloc("btok", [128, 2, D], BF16)
    balloc("w1a", [128, 2, KD, HB], BF16)
    balloc("w1b", [128, 2, KD, HB], BF16)
    balloc("w2sb", [128, KI, D], BF16)
    balloc("hfm", [128, KI, LCAP], BF16)
    balloc("glsc", [128, 2, 512])
    balloc("expo", [128, 2, D])

    NIW = LCAP // 16
    nsg = 2 if LCAP > 512 else 1
    SGS = ((0, 512), (512, LCAP - 512)) if LCAP > 512 else ((0, LCAP),)
    W1PB = NHB * NSB * nsg

    with nc.Block("b_exp") as blk:
        @blk.gpsimd
        def _(g):
            b.wait(g, "lists")
            b.wait(g, "yzw")
            for e in range(E):
                r = e % 2
                for sc_ in range(NSC):
                    cc = e * NSC + sc_
                    if cc >= 2:
                        g.wait_ge(b.sem("btr"), 16 * (cc - 1))
                    b.inc(g.indirect_dma_start(
                        H["btok"][:, cc % 2, :], None, DR["x2bf"].ap(),
                        IOA(H["tok32"][:, cc:cc + 1], 0)), "ggx", 16)
                for hb in range(NHB):
                    idx = e * NHB + hb
                    rw = idx % 2
                    if idx >= 2:
                        g.wait_ge(b.sem("w1mm"), (idx - 1) * NSB * nsg)
                    b.inc(g.dma_start(
                        out=H["w1a"][:, rw, :, :],
                        in_=IN["W1"].ap()[e].rearrange(
                            "(c p) i -> p c i", p=128)[:, :, HB * hb:
                                                       HB * (hb + 1)]),
                        "w1l", 16)
                    b.inc(g.dma_start(
                        out=H["w1b"][:, rw, :, :],
                        in_=IN["W1"].ap()[e].rearrange(
                            "(c p) i -> p c i", p=128)[:, :, II + HB * hb:
                                                       II + HB * (hb + 1)]),
                        "w1l", 16)
                if e >= 1:
                    g.wait_ge(b.sem("w2mm"), e)
                b.inc(g.dma_start(out=H["w2sb"][:, :, :],
                                  in_=IN["W2"].ap()[e].rearrange(
                                      "(c p) d -> p c d", p=128)), "w2l", 16)
                for sc_ in range(NSC):
                    cc = e * NSC + sc_
                    g.wait_ge(b.sem("eev"), cc + 1)
                    b.inc(g.indirect_dma_start(
                        DR["y"].ap(), IOA(H["tok32"][:, cc:cc + 1], 0),
                        H["expo"][:, cc % 2, :], None,
                        compute_op=ALU.add), "scat", 16)

        @blk.sync
        def _(sy):
            for e in range(E):
                for sc_ in range(NSC):
                    cc = e * NSC + sc_
                    sy.wait_ge(b.sem("ggx"), 16 * (cc + 1))
                    if cc >= 2:
                        pass
                    if e >= 2 and sc_ == 0:
                        sy.wait_ge(b.sem("w1mm"), (e - 1) * W1PB)
                    b.inc(sy.dma_start_transpose(
                        H["batch"][:, e % 2, :, 128 * sc_:128 * (sc_ + 1)],
                        sap(H["btok"], (cc % 2) * D,
                            [[2 * D, 128], [128, KD], [1, 128]])), "btr", 16)

        @blk.tensor
        def _(t):
            b.wait(t, "yzw")
            for e in range(E):
                r = e % 2
                t.wait_ge(b.sem("btr"), 16 * NSC * (e + 1))
                for hb in range(NHB):
                    idx = e * NHB + hb
                    rw = idx % 2
                    t.wait_ge(b.sem("w1l"), 32 * (idx + 1))
                    for sub in range(NSB):
                        ic = hb * NSB + sub
                        for sg, (s0, sw) in enumerate(SGS):
                            qq = (idx * NSB + sub) * nsg + sg
                            if qq >= 2:
                                t.wait_ge(b.sem("glu"), qq - 1)
                            for kc in range(KD):
                                t.matmul(PS["psA"][:, qq % 2, 0:sw],
                                         H["w1a"][:, rw, kc,
                                                  128 * sub:128 * (sub + 1)],
                                         H["batch"][:, r, kc, s0:s0 + sw],
                                         start=(kc == 0), stop=(kc == KD - 1))
                            for kc in range(KD):
                                mm = t.matmul(PS["psB"][:, qq % 2, 0:sw],
                                              H["w1b"][:, rw, kc,
                                                       128 * sub:128 * (sub + 1)],
                                              H["batch"][:, r, kc, s0:s0 + sw],
                                              start=(kc == 0),
                                              stop=(kc == KD - 1))
                            b.inc(mm, "w1mm", 1)
                t.wait_ge(b.sem("w2l"), 16 * (e + 1))
                t.wait_ge(b.sem("glu"), (e + 1) * W1PB)
                for sc_ in range(NSC):
                    for nh in range(NB):
                        psn = "psC" if nh % 2 == 0 else "psD"
                        if e * NSC + sc_ >= 2:
                            t.wait_ge(b.sem("eev"), e * NSC + sc_ - 1)
                        for ic in range(KI):
                            mm = t.matmul(PS[psn][:, sc_ % 2, :],
                                          H["hfm"][:, ic,
                                                   128 * sc_:128 * (sc_ + 1)],
                                          H["w2sb"][:, ic,
                                                    512 * nh:512 * (nh + 1)],
                                          start=(ic == 0), stop=(ic == KI - 1))
                    b.inc(mm, "w2sc", 1)
                b.inc(t.nop(), "w2mm", 1)

        @blk.scalar
        def _(sc):
            for e in range(E):
                for hb in range(NHB):
                    for sub in range(NSB):
                        for sg, (s0, sw) in enumerate(SGS):
                            qq = ((e * NHB + hb) * NSB + sub) * nsg + sg
                            sc.wait_ge(b.sem("w1mm"), qq + 1)
                            if qq >= 2:
                                sc.wait_ge(b.sem("glu"), qq - 1)
                            b.inc(sc.activation(H["glsc"][:, qq % 2, 0:sw],
                                                PS["psA"][:, qq % 2, 0:sw],
                                                AF.Gelu), "gelu", 1)

        @blk.vector
        def _(v):
            v = VG(v)
            for e in range(E):
                r = e % 2
                for hb in range(NHB):
                    for sub in range(NSB):
                        ic = hb * NSB + sub
                        for sg, (s0, sw) in enumerate(SGS):
                            qq = ((e * NHB + hb) * NSB + sub) * nsg + sg
                            v.wait_ge(b.sem("gelu"), qq + 1)
                            b.inc(v.tensor_mul(H["hfm"][:, ic, s0:s0 + sw],
                                               H["glsc"][:, qq % 2, 0:sw],
                                               PS["psB"][:, qq % 2, 0:sw]),
                                  "glu", 1)
                for sc_ in range(NSC):
                    q_ = e * NSC + sc_
                    v.wait_ge(b.sem("w2sc"), q_ + 1)
                    if q_ >= 2:
                        v.wait_ge(b.sem("scat"), 16 * (q_ - 1))
                    gcol = H["gate_bs"][:, q_:q_ + 1]
                    for nh in range(NB):
                        psn = "psC" if nh % 2 == 0 else "psD"
                        i = v.tensor_scalar_mul(
                            H["expo"][:, q_ % 2, 512 * nh:512 * (nh + 1)],
                            PS[psn][:, sc_ % 2, :], gcol)
                    b.inc(i, "eev", 1)

    for g_ in reversed(GB):
        g_.__exit__(None, None, None)
    GF = []
    def falloc(name, shp, dt=F32):
        g_ = nc.sbuf_tensor(name, shp, dt)
        h_ = g_.__enter__()
        GF.append(g_)
        H[name] = h_
        return h_
    falloc("fy", [128, 2, D])
    falloc("fx", [128, 2, D])
    falloc("fsc", [128, 2, D])

    with nc.Block("b_fin") as blk:
        bxld = b.val("xld")
        bsq = b.val("sqt1")
        bvs = b.val("vs1")
        bvb = b.val("vb1")
        bx2f = b.val("x2fs")
        bx2w = b.val("x2w")

        @blk.sync
        def _(sy):
            def owrite(tw):
                sy.wait_ge(b.sem("x2fs"), bx2f + tw + 1)
                b.inc(sy.dma_start(out=out_ext[128 * tw:128 * (tw + 1), :],
                                   in_=H["fsc"][:, tw % 2, :]), "x2w", 16)
            sy.wait_ge(b.sem("scat"), 16 * E * NSC)
            for tt in range(NT):
                if tt >= 2:
                    sy.wait_ge(b.sem("vs1"), bvs + tt - 1)
                    sy.wait_ge(b.sem("x2fs"), bx2f + tt - 1)
                b.inc(sy.dma_start(out=H["fy"][:, tt % 2, :],
                                   in_=DR["y"][128 * tt:128 * (tt + 1), :]),
                      "xld", 16)
                b.inc(sy.dma_start(out=H["fx"][:, tt % 2, :],
                                   in_=DR["x2f"][128 * tt:128 * (tt + 1), :]),
                      "xld", 16)
                if tt >= 2:
                    owrite(tt - 2)
                sy.wait_ge(b.sem("vs1"), bvs + tt + 1)
                b.inc(sy.dma_start(out=H["mu_s"][:, 0 + (tt % 2):1 + (tt % 2)],
                                   in_=H["mu"][:, 4 + (tt % 2):5 + (tt % 2)]),
                      "vb1", 16)
            for tw in (NT - 2, NT - 1):
                owrite(tw)
            sy.wait_ge(b.sem("x2w"), bx2w + 16 * NT)

        @blk.vector
        def _(v):
            v = VG(v)
            for tt in range(NT):
                r = tt % 2
                v.wait_ge(b.sem("xld"), bxld + 32 * (tt + 1))
                v.scalar_tensor_tensor(
                    H["fy"][:, r, :], H["fx"][:, r, :], 1.0, H["fy"][:, r, :],
                    ALU.mult, ALU.add)
                v.reduce_sum(H["mu"][:, 2 + r:3 + r], H["fy"][:, r, :],
                             axis=AL.X)
                v.tensor_scalar_mul(H["mu"][:, 2 + r:3 + r],
                                    H["mu"][:, 2 + r:3 + r], -1.0 / D)
                v.tensor_tensor(H["fx"][:, r, :], H["fy"][:, r, :],
                                H["fy"][:, r, :], ALU.mult)
                v.reduce_sum(H["mu"][:, 6 + r:7 + r], H["fx"][:, r, :],
                             axis=AL.X)
                v.tensor_scalar(H["mu"][:, 4 + r:5 + r],
                                H["mu"][:, 6 + r:7 + r], 1.0 / D, EPS,
                                ALU.mult, ALU.add)
                v.tensor_tensor(H["mu"][:, 0 + r:1 + r], H["mu"][:, 2 + r:3 + r],
                                H["mu"][:, 2 + r:3 + r], ALU.mult)
                i = v.tensor_tensor(H["mu"][:, 4 + r:5 + r],
                                    H["mu"][:, 4 + r:5 + r],
                                    H["mu"][:, 0 + r:1 + r], ALU.subtract)
                b.inc(i, "vs1", 1)
                if tt >= 2:
                    v.wait_ge(b.sem("x2w"), bx2w + 16 * (tt - 1))
                v.tensor_scalar(H["fsc"][:, r, :], H["fy"][:, r, :],
                                H["mu"][:, 2 + r:3 + r], None, ALU.add)
                v.wait_ge(b.sem("sqt1"), bsq + tt + 1)
                v.reciprocal(H["mu"][:, 8 + r:9 + r], H["mu_s"][:, 2 + r:3 + r])
                i = v.tensor_scalar(H["fsc"][:, r, :], H["fsc"][:, r, :],
                                    H["mu"][:, 8 + r:9 + r], None, ALU.mult)
                b.inc(i, "x2fs", 1)

        @blk.scalar
        def _(sc):
            for tt in range(NT):
                r = tt % 2
                sc.wait_ge(b.sem("vb1"), bvb + 16 * (tt + 1))
                b.inc(sc.activation(H["mu_s"][:, 2 + r:3 + r],
                                    H["mu_s"][:, 0 + r:1 + r], AF.Sqrt),
                      "sqt1", 1)

    return dict(nc=nc, b=b, H=H, PS=PS, IN=IN, DR=DR, CN=CN,
                out_ext=out_ext, P=P)

from concourse.bass_utils import run_bass_kernel_spmd

_PROG_CACHE = {}


def _get_prog():
    key = "full"
    if key not in _PROG_CACHE:
        _PROG_CACHE[key] = build_program(default_params())
    return _PROG_CACHE[key]


def kernel(L0_attn_x, L0_attn_Wq, L0_attn_bq, L0_attn_Wk, L0_attn_bk,
           L0_attn_Wv, L0_attn_bv, L0_attn_Wg, L0_attn_bg, L0_attn_Wo,
           L0_attn_bo, L0_attn_gamma, L0_attn_beta, L0_ffn_Wg, L0_ffn_W1,
           L0_ffn_W2):
    P = default_params()
    NC, G, T = P["NCORES"], P["GROUP"], P["T"]
    prog = _get_prog()
    nc = prog["nc"]

    x = np.ascontiguousarray(np.asarray(L0_attn_x, np.float32)).reshape(-1, P["D"])
    shared = {
        "Wq": np.ascontiguousarray(L0_attn_Wq, np.float32),
        "bq": np.ascontiguousarray(L0_attn_bq, np.float32),
        "Wk": np.ascontiguousarray(L0_attn_Wk, np.float32),
        "bk": np.ascontiguousarray(L0_attn_bk, np.float32),
        "Wv": np.ascontiguousarray(L0_attn_Wv, np.float32),
        "bv": np.ascontiguousarray(L0_attn_bv, np.float32),
        "Wg": np.ascontiguousarray(L0_attn_Wg, np.float32),
        "bg": np.ascontiguousarray(L0_attn_bg, np.float32),
        "Wo": np.ascontiguousarray(L0_attn_Wo, np.float32),
        "bo": np.ascontiguousarray(L0_attn_bo, np.float32),
        "gamma": np.ascontiguousarray(L0_attn_gamma, np.float32),
        "beta": np.ascontiguousarray(L0_attn_beta, np.float32),
        "Wge": np.ascontiguousarray(L0_ffn_Wg, np.float32),
        "W1": np.ascontiguousarray(L0_ffn_W1, np.float32),
        "W2": np.ascontiguousarray(L0_ffn_W2, np.float32),
    }
    in_maps = []
    for c in range(NC):
        m = dict(shared)
        m["x"] = np.ascontiguousarray(x[c * T:(c + 1) * T])
        r = c % G
        m3 = np.zeros((128, max(G - 1, 1)), np.float32)
        for j in range(G - 1):
            if j < r:
                m3[:, j] = 1.0
        m["mask3"] = m3
        rm = np.zeros((NC, 1), np.float32)
        rm[:c, 0] = 1.0
        m["rankm"] = rm
        in_maps.append(m)
    res = run_bass_kernel_spmd(nc, in_maps, core_ids=list(range(NC)))
    out = np.concatenate([np.asarray(r_["out"]) for r_ in res.results], axis=0)
    return out.reshape(np.asarray(L0_attn_x).shape).astype(np.float32)
